# version 03 (final)
# baseline (speedup 1.0000x reference)
import numpy as np
import concourse.bass as bass
import concourse.mybir as mybir
from concourse.bass_utils import run_bass_kernel_spmd

F32 = mybir.dt.float32
AX = mybir.AxisListType.X
OP = mybir.AluOpType
AF = mybir.ActivationFunctionType

D = 4096
NT = 8
TOK = 128
SC0 = 1.0 / float(np.float32(np.sqrt(128.0)))
SC1 = 0.125
EPS = 1e-5
NEGM = -1.0e9
RG = [[0, 1, 2, 3], [4, 5, 6, 7]]


class Ctr:
    def __init__(self, nc, name):
        self.h = nc.alloc_semaphore(name=name)
        self.n = 0


class Slot:
    def __init__(self):
        self.dep = None

    def take(self):
        d, self.dep = self.dep, None
        return d

    def set(self, ev):
        self.dep = ev


class EM:
    def __init__(self, nc):
        self.nc = nc
        self.qs = {k: [] for k in ("sy", "te", "ve", "se", "gp")}
        self.LD = Ctr(nc, "LD")
        self.GP = Ctr(nc, "GP")
        self.PE = Ctr(nc, "PE")
        self.VE = Ctr(nc, "VE")
        self.SE = Ctr(nc, "SE")
        self.CC = Ctr(nc, "CC")

    def emit(self, eng, fn, waits=(), bump=None, bumpk=1):
        waits = tuple(w for w in waits if w is not None)
        if eng == "ve":
            waits = waits + ((self.VE, self.VE.n),)
            if bump is None:
                bump = self.VE
        elif eng == "se":
            waits = waits + ((self.SE, self.SE.n),)
            if bump is None:
                bump = self.SE
        if bump is not None:
            bump.n += bumpk
        ev = (bump, bump.n) if bump is not None else None

        def f(e, fn=fn, waits=waits, bump=bump, bumpk=bumpk):
            for (c, v) in waits:
                e.wait_ge(c.h, v)
            ins = fn(e)
            if bump is not None:
                ins.then_inc(bump.h, bumpk)
        self.qs[eng].append(f)
        return ev

    def dma(self, q, dst, src, waits=(), sem=None):
        c = sem if sem is not None else (self.LD if q == "sy" else self.GP)
        return self.emit(q, lambda e, dst=dst, src=src: e.dma_start(out=dst, in_=src),
                         waits=waits, bump=c, bumpk=16)

    def wait(self, eng, *evs):
        evs = tuple(ev for ev in evs if ev is not None)
        if evs:
            def f(e, evs=evs):
                for (c, v) in evs:
                    e.wait_ge(c.h, v)
            self.qs[eng].append(f)

    def now(self, ctr):
        return (ctr, ctr.n)

    def barrier(self):
        evs = {"te": self.now(self.PE), "ve": self.now(self.VE), "se": self.now(self.SE)}
        for eng, others in (("te", ("ve", "se")), ("ve", ("te", "se")), ("se", ("te", "ve"))):
            self.wait(eng, *[evs[o] for o in others])


def build_program(upto=5):
    nc = bass.Bass("TRN2", target_bir_lowering=False)

    def din(name, shape):
        return nc.dram_tensor(name, shape, F32, kind="ExternalInput").ap()

    x_in = din("x", [NT * TOK, D])
    wq = din("wq", [D, D])
    wk = din("wk", [D, 1024])
    wv = din("wv", [D, 1024])
    wo_ = din("wo", [D, D])
    fW = din("fW", [D, D])
    facat = din("facat", [D, 256])
    fbcat = din("fbcat", [256, D])
    probsr = din("probsr", [256, 1])
    fbbc_d = din("fbbc", [128, D])
    wqkv = din("wqkv", [D, 1536])
    wot = din("wot", [1024, D])
    qc1_d = din("qc1", [3 * 1024, 1024])
    qc2_d = din("qc2", [3 * 1024, 1024])
    kc1_d = din("kc1", [3 * 256, 256])
    kc2_d = din("kc2", [3 * 256, 256])
    qc1b = din("qc1b", [1024, 1])
    qc2b = din("qc2b", [1024, 1])
    kc1b = din("kc1b", [256, 1])
    kc2b = din("kc2b", [256, 1])
    ktemp = din("ktemp", [128, 1])
    epsv = din("epsv", [128, 1])
    lA_d = din("lA", [128, 64])
    lB_d = din("lB", [128, 64])
    lbias_d = din("lbias", [128, D])
    lgam_d = din("lgam", [128, D])
    lbet_d = din("lbet", [128, D])
    r0 = {s: din("r0" + s, [NT * TOK, 64]) for s in ("ch", "cl", "sh", "sl")}
    r1 = {s: din("r1" + s, [NT * TOK, 32]) for s in ("ch", "cl", "sh", "sl")}
    maskin = din("maskin", [128, 512])
    halosel = din("halosel", [128, 32])
    identin = din("identin", [128, 128])

    out = nc.dram_tensor("out", [NT * TOK, D], F32, kind="ExternalOutput").ap()
    dbg = {}
    if upto == 0:
        dbg["ag0k"] = nc.dram_tensor("dbg_ag0k", [4096, 1024], F32, kind="ExternalOutput").ap()
        dbg["ag0v"] = nc.dram_tensor("dbg_ag0v", [4096, 1024], F32, kind="ExternalOutput").ap()
        dbg["ag0vin"] = nc.dram_tensor("dbg_ag0vin", [1024, 1024], F32, kind="ExternalOutput").ap()
    if upto == 1:
        dbg["x1"] = nc.dram_tensor("dbg_x1", [NT * TOK, D], F32, kind="ExternalOutput").ap()
        dbg["O"] = nc.dram_tensor("dbg_O", [NT * TOK, D], F32, kind="ExternalOutput").ap()

    if upto == 4:
        dbg["agk1"] = nc.dram_tensor("dbg_agk1", [1024, 1024], F32, kind="ExternalOutput").ap()
        dbg["agv1"] = nc.dram_tensor("dbg_agv1", [4096, 256], F32, kind="ExternalOutput").ap()
        dbg["O1"] = nc.dram_tensor("dbg_O1", [NT * TOK, 1024], F32, kind="ExternalOutput").ap()
    if upto == 3:
        dbg["qkv"] = nc.dram_tensor("dbg_qkv", [NT * TOK, 1536], F32, kind="ExternalOutput").ap()
        dbg["halo"] = nc.dram_tensor("dbg_halo", [128, 1536], F32, kind="ExternalOutput").ap()
        dbg["haloin"] = nc.dram_tensor("dbg_haloin", [32, 1536], F32, kind="ExternalOutput").ap()
        dbg["haloin2"] = nc.dram_tensor("dbg_haloin2", [32, 1536], F32, kind="ExternalOutput").ap()

    ag0k_in = nc.dram_tensor("ag0k_in", [1024, 1024], F32).ap()
    ag0k_out = nc.dram_tensor("ag0k_out", [4096, 1024], F32).ap()
    ag0v_in = nc.dram_tensor("ag0v_in", [1024, 1024], F32).ap()
    ag0v_out = nc.dram_tensor("ag0v_out", [4096, 1024], F32).ap()
    ag1a_in = nc.dram_tensor("ag1a_in", [32, 1536], F32).ap()
    ag1a_out = nc.dram_tensor("ag1a_out", [128, 1536], F32).ap()
    ag1k_in = nc.dram_tensor("ag1k_in", [256, 1024], F32).ap()
    ag1k_out = nc.dram_tensor("ag1k_out", [1024, 1024], F32).ap()
    ag1v_in = nc.dram_tensor("ag1v_in", [1024, 256], F32).ap()
    ag1v_out = nc.dram_tensor("ag1v_out", [4096, 256], F32).ap()
    spx2 = nc.dram_tensor("spx2", [NT * TOK, D], F32).ap()
    spx3 = nc.dram_tensor("spx3", [NT * TOK, D], F32).ap()

    def sbuf(name, cols):
        return nc.alloc_sbuf_tensor(name, [128, cols], F32).ap()

    AR1 = sbuf("AR1", 16384)
    AR2 = sbuf("AR2", 16384)
    AR3 = sbuf("AR3", 8192)
    c_id = sbuf("c_id", 128)
    c_mask = sbuf("c_mask", 512)
    c_cos0 = sbuf("c_cos0", 512)
    c_sin0 = sbuf("c_sin0", 512)
    c_cos1 = sbuf("c_cos1", 256)
    c_sin1 = sbuf("c_sin1", 256)
    c_misc = sbuf("c_misc", 32)
    c_lA = sbuf("c_lA", 64)
    c_lB = sbuf("c_lB", 64)
    c_hsel = sbuf("c_hsel", 32)
    st = sbuf("st", 32)
    stm = sbuf("stm", 64)
    stm1 = sbuf("stm1", 256)
    stm1i = sbuf("stm1i", 128)

    pb = [nc.alloc_psum_tensor("pb%d" % i, [128, 512], F32).ap() for i in range(8)]
    ps = [Slot() for _ in range(8)]

    em = EM(nc)
    LD, GP, PE, VE, SE, CC = em.LD, em.GP, em.PE, em.VE, em.SE, em.CC
    LDC = Ctr(nc, "LDC")
    LDA = Ctr(nc, "LDA")
    LDB = Ctr(nc, "LDB")
    LDW = [Ctr(nc, "LDW%d" % i) for i in range(4)]
    LDK = [Ctr(nc, "LDK%d" % i) for i in range(2)]
    LDV = [Ctr(nc, "LDV%d" % i) for i in range(2)]
    GPK = Ctr(nc, "GPK")
    GPD = Ctr(nc, "GPD")

    M_KTEMP, M_PROBS, M_QC1B, M_QC2B, M_KC1B, M_KC2B = 0, 1, 3, 11, 19, 21

    def tp(bank_ap, src, start=True, stop=True):
        p_ = src.shape[0]
        return lambda e, bank_ap=bank_ap, src=src, start=start, stop=stop, p_=p_: e.matmul(
            bank_ap, src, c_id[0:p_, 0:p_], is_transpose=True, start=start, stop=stop)

    em.dma("sy", c_id[:, :], identin, sem=LDC)
    em.dma("sy", c_mask[:, :], maskin, sem=LDC)
    tabspec = ((c_cos0, r0, 64, "c", 0), (c_sin0, r0, 64, "s", 512),
               (c_cos1, r1, 32, "c", 1024), (c_sin1, r1, 32, "s", 1280))
    for (dst, rr, w, kind, off) in tabspec:
        for t in range(NT):
            em.dma("sy", dst[:, w * t:w * (t + 1)], rr[kind + "h"][TOK * t:TOK * (t + 1), :],
                   sem=LDC)
            em.dma("sy", AR3[:, off + w * t:off + w * (t + 1)],
                   rr[kind + "l"][TOK * t:TOK * (t + 1), :], sem=LDC)
    em.dma("sy", c_misc[:, M_KTEMP:M_KTEMP + 1], ktemp[:, :], sem=LDC)
    em.dma("sy", c_misc[:, 23:24], epsv[:, :], sem=LDC)
    for (base, src_, n_) in ((M_PROBS, probsr, 2), (M_QC1B, qc1b, 8), (M_QC2B, qc2b, 8),
                             (M_KC1B, kc1b, 2), (M_KC2B, kc2b, 2)):
        for i in range(n_):
            em.dma("sy", c_misc[:, base + i:base + i + 1], src_[128 * i:128 * (i + 1), :], sem=LDC)
    em.dma("sy", c_lA[:, :], lA_d, sem=LDC)
    em.dma("sy", c_lB[:, :], lB_d, sem=LDC)
    ev_const = em.dma("sy", c_hsel[:, :], halosel, sem=LDC)
    ev_tabs = None
    for (dst, rr, w, kind, off) in tabspec:
        ev_tabs = em.emit("ve", lambda e, dst=dst, w=w, off=off: e.tensor_tensor(
            dst[:, 0:8 * w], dst[:, 0:8 * w], AR3[:, off:off + 8 * w], OP.add),
            waits=[ev_const], bump=VE)
    ev_id = ev_mask = ev_misc = ev_lab = ev_hsel = ev_const
    ar3_free = ev_tabs

    def transpose_cols(src, dst, nchunk, pre=(), banks=(0, 1), dst_stride=128, dst_off=0):
        last = None
        for k in range(nchunk):
            b = banks[k % len(banks)]
            pe_ev = em.emit("te", tp(pb[b][:, 0:128], src[:, 128 * k:128 * (k + 1)]),
                            waits=[ps[b].take(), ev_id] + (list(pre) if k == 0 else []), bump=PE)
            dcol = dst_off + dst_stride * k
            last = em.emit("ve", lambda e, b=b, dst=dst, dcol=dcol: e.tensor_copy(
                dst[:, dcol:dcol + 128], pb[b][:, 0:128]), waits=[pe_ev], bump=VE)
            ps[b].set(last)
        return last

    def ln_inplace(buf, sq, gamma=None, beta=None, pre=()):
        em.emit("ve", lambda e: e.reduce_sum(st[:, 21:22], buf, AX), waits=pre, bump=VE)
        vm = em.emit("ve", lambda e: e.tensor_scalar_mul(st[:, 22:23], st[:, 21:22], 1.0 / D), bump=VE)
        vs = em.emit("ve", lambda e: e.tensor_scalar(buf, buf, st[:, 22:23], None, OP.subtract),
                     waits=[vm], bump=VE)
        em.emit("se", lambda e: e.activation(sq, buf, AF.Square, accum_out=st[:, 23:24]),
                waits=[vs], bump=SE)
        sstd = em.emit("se", lambda e: e.activation(st[:, 24:25], st[:, 23:24], AF.Sqrt,
                                                    bias=c_misc[:, 23:24], scale=1.0 / D), bump=SE)
        vrec = em.emit("ve", lambda e: e.reciprocal(st[:, 21:22], st[:, 24:25]),
                       waits=[sstd], bump=VE)
        v = em.emit("ve", lambda e: e.tensor_scalar(buf, buf, st[:, 21:22], None, OP.mult),
                    waits=[vrec], bump=VE)
        if gamma is not None:
            em.emit("ve", lambda e: e.tensor_tensor(buf, buf, gamma, OP.mult), bump=VE)
            v = em.emit("ve", lambda e: e.tensor_tensor(buf, buf, beta, OP.add), bump=VE)
        return v

    def rope(src, dst, tmp, costab, sintab, H, hd, pre=()):
        h2 = hd // 2
        sv = src.rearrange("p (h j two) -> p h j two", h=H, j=h2, two=2)
        dv = dst.rearrange("p (h j two) -> p h j two", h=H, j=h2, two=2)
        se_, so_ = sv[:, :, :, 0:1].squeeze(3), sv[:, :, :, 1:2].squeeze(3)
        de_, do_ = dv[:, :, :, 0:1].squeeze(3), dv[:, :, :, 1:2].squeeze(3)
        cb = costab.unsqueeze(1).broadcast_to([128, H, h2])
        sbb = sintab.unsqueeze(1).broadcast_to([128, H, h2])
        t1 = tmp[:, 0:H * h2].rearrange("p (h j) -> p h j", h=H, j=h2)
        t2 = tmp[:, H * h2:2 * H * h2].rearrange("p (h j) -> p h j", h=H, j=h2)
        em.emit("ve", lambda e: e.tensor_tensor(t1, se_, cb, OP.mult), waits=pre, bump=VE)
        em.emit("ve", lambda e: e.tensor_tensor(t2, so_, sbb, OP.mult), bump=VE)
        em.emit("ve", lambda e: e.tensor_tensor(de_, t1, t2, OP.subtract), bump=VE)
        em.emit("ve", lambda e: e.tensor_tensor(t1, se_, sbb, OP.mult), bump=VE)
        em.emit("ve", lambda e: e.tensor_tensor(t2, so_, cb, OP.mult), bump=VE)
        return em.emit("ve", lambda e: e.tensor_tensor(do_, t1, t2, OP.add), bump=VE)

    wb = [AR3[:, 512 * i:512 * (i + 1)] for i in range(4)]
    wsl = [Slot() for _ in range(4)]
    SI = [0]

    def mm_chain(banks, chunks, ncols, pre=()):
        nch = len(chunks)
        for ci, (wsrc, lhss, sc) in enumerate(chunks):
            s = SI[0] % 4
            SI[0] += 1
            evw = em.dma("sy", wb[s][:, 0:ncols], wsrc, waits=[wsl[s].take()], sem=LDW[s])
            if sc is not None:
                evw = em.emit("ve", lambda e, s=s, sc=sc, ncols=ncols: e.tensor_scalar(
                    wb[s][:, 0:ncols], wb[s][:, 0:ncols], c_misc[:, sc:sc + 1], None, OP.mult),
                    waits=[evw, ev_misc], bump=VE)
            for bi, bank in enumerate(banks):
                w = [evw] + (list(pre) if (ci == 0 and bi == 0) else [])
                lhs = lhss[bi]
                em.emit("te", lambda e, bank=bank, lhs=lhs, s=s, ci=ci, nch=nch, ncols=ncols: e.matmul(
                    pb[bank][:, 0:ncols], lhs, wb[s][:, 0:ncols],
                    start=(ci == 0), stop=(ci == nch - 1)),
                    waits=w, bump=(PE if bi == len(banks) - 1 else None))
            wsl[s].set(em.now(PE))
        return em.now(PE)

    for s in range(4):
        wsl[s].set(ar3_free)

    xt = AR2[:, 0:4096]
    xf = AR1[:, 0:4096]
    kvbuf = AR1[:, 4096:5120]
    krop = AR1[:, 5120:6144]
    rtmp0 = AR1[:, 6144:7168]
    kstg = AR1[:, 7168:8192]
    xsl = Slot()
    kvsl = Slot()
    kropsl = Slot()
    rtmpsl = Slot()
    kstgsl = Slot()

    for t in range(NT):
        ev_x = em.dma("sy", xt, x_in[TOK * t:TOK * (t + 1), :], waits=[xsl.take()], sem=LDA)
        xf_ev = transpose_cols(xt, xf, 32, pre=[ev_x])
        xsl.set(em.now(PE))
        for (nm, W) in (("k", wk), ("v", wv)):
            em.wait("te", ps[4].take(), ps[5].take())
            for k in range(32):
                s0, s1 = SI[0] % 4, (SI[0] + 1) % 4
                SI[0] += 2
                ev0 = em.dma("sy", wb[s0], W[128 * k:128 * (k + 1), 0:512],
                             waits=[wsl[s0].take()], sem=LDW[s0])
                ev1 = em.dma("sy", wb[s1], W[128 * k:128 * (k + 1), 512:1024],
                             waits=[wsl[s1].take()], sem=LDW[s1])
                for (n, bank, evw, s) in ((0, 4, ev0, s0), (1, 5, ev1, s1)):
                    w = [evw] + ([xf_ev] if k == 0 and n == 0 else [])
                    em.emit("te", lambda e, bank=bank, k=k, s=s: e.matmul(
                        pb[bank], xf[:, 128 * k:128 * (k + 1)], wb[s],
                        start=(k == 0), stop=(k == 31)), waits=w, bump=(PE if n == 1 else None))
                if True:
                    wsl[s0].set(em.now(PE))
                    wsl[s1].set(em.now(PE))
            mm_ev = em.now(PE)
            evac = None
            for n in range(2):
                evac = em.emit("ve", lambda e, n=n: e.tensor_copy(
                    kvbuf[:, 512 * n:512 * (n + 1)], pb[4 + n]),
                    waits=[mm_ev, kvsl.take()], bump=VE)
                ps[4 + n].set(evac)
            if nm == "k":
                rp_ev = rope(kvbuf, krop, rtmp0, c_cos0[:, 64 * t:64 * (t + 1)],
                             c_sin0[:, 64 * t:64 * (t + 1)], 8, 128,
                             pre=[evac, ev_tabs, kropsl.take(), rtmpsl.take()])
                kvsl.set(em.now(VE))
                for ch in range(8):
                    b = 6 + (ch % 2)
                    pe_ev = em.emit("te", tp(pb[b][:, 0:128], krop[:, 128 * ch:128 * (ch + 1)]),
                                    waits=[rp_ev, ps[b].take()] + ([kstgsl.take()] if ch == 0 else []),
                                    bump=PE)
                    st_ev = em.emit("ve", lambda e, b=b, ch=ch: e.tensor_copy(
                        kstg[:, 128 * (ch % 8):128 * (ch % 8) + 128], pb[b][:, 0:128]),
                        waits=[pe_ev], bump=VE)
                    ps[b].set(st_ev)
                    em.dma("gp", ag0k_in[128 * ch:128 * (ch + 1), TOK * t:TOK * (t + 1)],
                           kstg[:, 128 * ch:128 * (ch + 1)], waits=[st_ev], sem=GPK)
                kropsl.set(em.now(PE))
                kstgsl.set(em.now(GPK))
                rtmpsl.set(em.now(VE))
            else:
                em.dma("gp", ag0v_in[TOK * t:TOK * (t + 1), :], kvbuf, waits=[evac])
                kvsl.set(em.now(GP))
    ev_ag0k = em.emit("gp", lambda e: e.collective_compute(
        "AllGather", OP.bypass, replica_groups=RG, ins=[ag0k_in], outs=[ag0k_out]),
        waits=[em.now(GP), em.now(GPK)], bump=CC)
    ev_ag0v = em.emit("gp", lambda e: e.collective_compute(
        "AllGather", OP.bypass, replica_groups=RG, ins=[ag0v_in], outs=[ag0v_out]),
        bump=CC)

    if upto == 0:
        ld_ev = em.wait("gp", ev_ag0v)
        for r_ in range(32):
            ev = em.dma("sy", AR2[:, 0:1024], ag0k_out[128 * r_:128 * (r_ + 1), :],
                        waits=[ev_ag0k, ev_ag0v, xsl.take()])
            sev = em.dma("gp", dbg["ag0k"][128 * r_:128 * (r_ + 1), :], AR2[:, 0:1024], waits=[ev])
            xsl.set(sev)
            ev = em.dma("sy", AR2[:, 0:1024], ag0v_out[128 * r_:128 * (r_ + 1), :], waits=[xsl.take()])
            sev = em.dma("gp", dbg["ag0v"][128 * r_:128 * (r_ + 1), :], AR2[:, 0:1024], waits=[ev])
            xsl.set(sev)
        for r_ in range(8):
            ev = em.dma("sy", AR2[:, 0:1024], ag0v_in[128 * r_:128 * (r_ + 1), :], waits=[xsl.take()])
            sev = em.dma("gp", dbg["ag0vin"][128 * r_:128 * (r_ + 1), :], AR2[:, 0:1024], waits=[ev])
            xsl.set(sev)
        for t in range(NT):
            ev = em.dma("sy", AR2[:, 0:4096], x_in[TOK * t:TOK * (t + 1), :], waits=[xsl.take()])
            sev = em.dma("gp", out[TOK * t:TOK * (t + 1), :], AR2[:, 0:4096], waits=[ev])
            xsl.set(sev)
        em.wait("gp", em.now(GP))

    spill_evs = [None] * NT
    if upto >= 1:
        racc = AR1[:, 0:8192]
        xf2 = AR1[:, 8192:16384]
        x_res = AR2[:, 0:4096]
        sqs = AR2[:, 4096:8192]
        ssb = [AR2[:, 4096 + 512 * i:4096 + 512 * (i + 1)] for i in range(2)]
        rtmp1 = AR2[:, 5120:6144]
        hbuf = AR2[:, 6144:6656]
        hfb = AR2[:, 6656:7168]
        qfb = AR2[:, 8192:9216]
        qz = AR2[:, 9216:11264]
        obuf = AR2[:, 11264:12288]
        fbbc = AR2[:, 12288:16384]
        ktb = [AR3[:, 2048 + 512 * i:2048 + 512 * (i + 1)] for i in range(2)]
        vtb = [AR3[:, 3072 + 512 * i:3072 + 512 * (i + 1)] for i in range(2)]
        pbuf = [AR3[:, 4096 + 512 * i:4096 + 512 * (i + 1)] for i in range(2)]
        ptsb = [AR3[:, 5120 + 512 * i:5120 + 512 * (i + 1)] for i in range(2)]
        kt_sl = [Slot(), Slot()]
        vt_sl = [Slot(), Slot()]
        pbuf_sl = [Slot(), Slot()]
        ptsb_sl = [Slot(), Slot()]
        ssb_sl = [Slot(), Slot()]
        qz_sl = Slot()
        qf_sl = Slot()
        o_sl = Slot()
        racc_sl = Slot()
        xf2_sl = Slot()
        xres_sl = Slot()
        sqs_sl = Slot()

        fbbc_ev = em.dma("sy", fbbc, fbbc_d, sem=LDC)
        first_kv = True
        for p in range(4):
            tiles = (2 * p, 2 * p + 1)
            tmax = tiles[-1]
            em.wait("te", xf2_sl.take())
            xf_evs = None
            for ti, t in enumerate(tiles):
                ev_x = em.dma("sy", xt, x_in[TOK * t:TOK * (t + 1), :], waits=[xsl.take()], sem=LDA)
                xf_evs = transpose_cols(xt, xf2, 32, pre=[ev_x], dst_off=4096 * ti)
                xsl.set(em.now(PE))
            for g in range(8):
                em.wait("te", ps[4].take(), ps[5].take())
                mm_ev = mm_chain(
                    (4, 5),
                    [(wq[128 * k:128 * (k + 1), 512 * g:512 * (g + 1)],
                      [xf2[:, 4096 * ti + 128 * k:4096 * ti + 128 * (k + 1)] for ti in range(2)],
                      None) for k in range(32)],
                    512, pre=[xf_evs])
                qev = [None, None]
                qz_guard = qz_sl.take()
                for ti in range(2):
                    qev[ti] = em.emit("ve", lambda e, ti=ti: e.tensor_copy(
                        qz[:, 512 * ti:512 * (ti + 1)], pb[4 + ti]),
                        waits=[mm_ev] + ([qz_guard] if ti == 0 else []), bump=VE)
                    ps[4 + ti].set(qev[ti])
                rp = [None, None]
                for ti, t in enumerate(tiles):
                    rp[ti] = rope(qz[:, 512 * ti:512 * (ti + 1)], qz[:, 1024 + 512 * ti:1024 + 512 * (ti + 1)],
                                  rtmp1, c_cos0[:, 64 * t:64 * (t + 1)], c_sin0[:, 64 * t:64 * (t + 1)],
                                  4, 128, pre=[qev[ti], ev_tabs])
                em.wait("te", qf_sl.take())
                for ti in range(2):
                    for h in range(4):
                        b = 6 + ((ti * 4 + h) % 2)
                        col = 1024 + 512 * ti + 128 * h
                        pe_ev = em.emit("te", tp(pb[b][:, 0:128], qz[:, col:col + 128]),
                                        waits=[rp[ti], ps[b].take()], bump=PE)
                        e_ev = em.emit("ve", lambda e, b=b, ti=ti, h=h: e.tensor_copy(
                            qfb[:, (ti * 4 + h) * 128:(ti * 4 + h) * 128 + 128], pb[b][:, 0:128]),
                            waits=[pe_ev], bump=VE)
                        ps[b].set(e_ev)
                qz_sl.set(em.now(PE))
                qf_ev = em.now(VE)

                unit = 0
                for j in range(0, tmax + 1):
                    par = j % 2
                    kt_ev = None
                    for rr in range(4):
                        w = []
                        if rr == 0:
                            w = [kt_sl[par].take()]
                            if first_kv:
                                w += [ev_ag0k, ev_ag0v]
                        kt_ev = em.dma("sy", ktb[par][:, 128 * rr:128 * (rr + 1)],
                                       ag0k_out[1024 * rr + 128 * g:1024 * rr + 128 * (g + 1),
                                                TOK * j:TOK * (j + 1)], waits=w, sem=LDK[par])
                    vt_ev = None
                    for rr in range(4):
                        vt_ev = em.dma("sy", vtb[par][:, 128 * rr:128 * (rr + 1)],
                                       ag0v_out[1024 * rr + TOK * j:1024 * rr + TOK * (j + 1),
                                                128 * g:128 * (g + 1)],
                                       waits=[vt_sl[par].take()] if rr == 0 else [], sem=LDV[par])
                    first_kv = False
                    for ti, t in enumerate(tiles):
                        if t < j:
                            continue
                        for h in range(4):
                            up = unit % 2
                            sb_, ob_, vb_ = up, 2 + up, 4 + up
                            sidx = 2 * (ti * 4 + h)
                            s_ev = em.emit("te", lambda e, sb_=sb_, ti=ti, h=h, par=par: e.matmul(
                                pb[sb_], qfb[:, (ti * 4 + h) * 128:(ti * 4 + h) * 128 + 128],
                                ktb[par]), waits=[kt_ev, qf_ev, ps[sb_].take()], bump=PE)
                            kt_sl[par].set(em.now(PE))
                            diag = (j == t)
                            if diag:
                                src = ssb[up]
                                src_ev = em.emit("ve", lambda e, up=up, sb_=sb_: e.tensor_tensor(
                                    ssb[up], pb[sb_], c_mask[:, :], OP.add),
                                    waits=[s_ev, ssb_sl[up].take(), ev_mask], bump=VE)
                                ps[sb_].set(src_ev)
                            else:
                                src = pb[sb_]
                                src_ev = s_ev
                            first = (j == 0)
                            if first:
                                vm = em.emit("ve", lambda e, src=src, sidx=sidx: e.reduce_max(
                                    stm[:, sidx:sidx + 1], src, AX), waits=[src_ev], bump=VE)
                                vnb = em.emit("ve", lambda e, sidx=sidx: e.tensor_scalar_mul(
                                    st[:, 19:20], stm[:, sidx:sidx + 1], -SC0), waits=[vm], bump=VE)
                                pexp = em.emit("se", lambda e, src=src, up=up: e.activation(
                                    pbuf[up], src, AF.Exp, bias=st[:, 19:20], scale=SC0,
                                    accum_out=st[:, 20:21]), waits=[vnb, pbuf_sl[up].take()], bump=SE)
                                em.emit("ve", lambda e, sidx=sidx: e.tensor_copy(
                                    stm[:, sidx + 1:sidx + 2], st[:, 20:21]), waits=[pexp], bump=VE)
                                salpha = None
                            else:
                                vm = em.emit("ve", lambda e, src=src: e.reduce_max(
                                    st[:, 16:17], src, AX), waits=[src_ev], bump=VE)
                                em.emit("ve", lambda e, sidx=sidx: e.tensor_tensor(
                                    st[:, 17:18], st[:, 16:17], stm[:, sidx:sidx + 1], OP.max), bump=VE)
                                vd = em.emit("ve", lambda e, sidx=sidx: e.tensor_tensor(
                                    st[:, 18:19], stm[:, sidx:sidx + 1], st[:, 17:18], OP.subtract),
                                    bump=VE)
                                vmu = em.emit("ve", lambda e, sidx=sidx: e.tensor_copy(
                                    stm[:, sidx:sidx + 1], st[:, 17:18]), bump=VE)
                                salpha = em.emit("se", lambda e: e.activation(
                                    st[:, 25:26], st[:, 18:19], AF.Exp, scale=SC0), waits=[vd], bump=SE)
                                vnb = em.emit("ve", lambda e: e.tensor_scalar_mul(
                                    st[:, 19:20], st[:, 17:18], -SC0), waits=[vmu], bump=VE)
                                pexp = em.emit("se", lambda e, src=src, up=up: e.activation(
                                    pbuf[up], src, AF.Exp, bias=st[:, 19:20], scale=SC0,
                                    accum_out=st[:, 20:21]), waits=[vnb, pbuf_sl[up].take()], bump=SE)
                                em.emit("ve", lambda e, sidx=sidx: e.scalar_tensor_tensor(
                                    stm[:, sidx + 1:sidx + 2], stm[:, sidx + 1:sidx + 2],
                                    st[:, 25:26], st[:, 20:21], OP.mult, OP.add),
                                    waits=[pexp, salpha], bump=VE)
                            if not diag:
                                ps[sb_].set((SE, SE.n))
                            else:
                                ssb_sl[up].set((SE, SE.n))
                            em.wait("te", ps[ob_].take())
                            for s4 in range(4):
                                em.emit("te", tp(pb[ob_][:, 128 * s4:128 * (s4 + 1)],
                                                 pbuf[up][:, 128 * s4:128 * (s4 + 1)],
                                                 start=(s4 == 0), stop=(s4 == 3)),
                                        waits=[pexp] if s4 == 0 else [],
                                        bump=(PE if s4 == 3 else None))
                            pt_pe = em.now(PE)
                            pbuf_sl[up].set(pt_pe)
                            ptev = em.emit("ve", lambda e, ob_=ob_, up=up: e.tensor_copy(
                                ptsb[up], pb[ob_]), waits=[pt_pe, ptsb_sl[up].take()], bump=VE)
                            ps[ob_].set(ptev)
                            em.wait("te", ps[vb_].take())
                            for s4 in range(4):
                                em.emit("te", lambda e, vb_=vb_, up=up, s4=s4, par=par: e.matmul(
                                    pb[vb_][:, 0:128], ptsb[up][:, 128 * s4:128 * (s4 + 1)],
                                    vtb[par][:, 128 * s4:128 * (s4 + 1)],
                                    start=(s4 == 0), stop=(s4 == 3)),
                                    waits=[ptev, vt_ev] if s4 == 0 else [],
                                    bump=(PE if s4 == 3 else None))
                            pv_pe = em.now(PE)
                            ptsb_sl[up].set(pv_pe)
                            vt_sl[par].set(pv_pe)
                            oc = (ti * 4 + h) * 128
                            if first:
                                ovev = em.emit("ve", lambda e, vb_=vb_, oc=oc: e.tensor_copy(
                                    obuf[:, oc:oc + 128], pb[vb_][:, 0:128]),
                                    waits=[pv_pe, o_sl.take()], bump=VE)
                            else:
                                ovev = em.emit("ve", lambda e, vb_=vb_, oc=oc: e.scalar_tensor_tensor(
                                    obuf[:, oc:oc + 128], obuf[:, oc:oc + 128], st[:, 25:26],
                                    pb[vb_][:, 0:128], OP.mult, OP.add),
                                    waits=[pv_pe, salpha], bump=VE)
                            ps[vb_].set(ovev)
                            unit += 1
                vdiv_pre = em.now(VE)
                vrec = em.emit("ve", lambda e: e.reciprocal(
                    stm[:, 33:48:2], stm[:, 1:16:2]), waits=[vdiv_pre], bump=VE)
                for ti in range(2):
                    for h in range(4):
                        sidx = 2 * (ti * 4 + h)
                        oc = (ti * 4 + h) * 128
                        em.emit("ve", lambda e, oc=oc, sidx=sidx: e.tensor_scalar(
                            obuf[:, oc:oc + 128], obuf[:, oc:oc + 128],
                            stm[:, 33 + sidx:34 + sidx], None, OP.mult),
                            waits=[vrec] if (ti == 0 and h == 0) else [], bump=VE)
                odiv_ev = em.now(VE)
                if upto == 1:
                    for ti, t in enumerate(tiles):
                        em.dma("gp", dbg["O"][TOK * t:TOK * (t + 1), 512 * g:512 * (g + 1)],
                               obuf[:, 512 * ti:512 * (ti + 1)], waits=[odiv_ev], sem=GPD)
                    o_sl.set((GPD, GPD.n))
                em.wait("te", qz_sl.take(), qf_sl.take())
                ofb = qz
                for ti in range(2):
                    for h in range(4):
                        b = 6 + ((ti * 4 + h) % 2)
                        oc = (ti * 4 + h) * 128
                        pe_ev = em.emit("te", tp(pb[b][:, 0:128], obuf[:, oc:oc + 128]),
                                        waits=[odiv_ev, ps[b].take()], bump=PE)
                        e_ev = em.emit("ve", lambda e, b=b, oc=oc: e.tensor_copy(
                            ofb[:, oc:oc + 128], pb[b][:, 0:128]), waits=[pe_ev], bump=VE)
                        ps[b].set(e_ev)
                o_sl.set(em.now(PE))
                of_ev = em.now(VE)
                for n in range(8):
                    em.wait("te", ps[4].take(), ps[5].take())
                    mm_ev = mm_chain(
                        (4, 5),
                        [(wo_[128 * (4 * g + h):128 * (4 * g + h) + 128, 512 * n:512 * (n + 1)],
                          [ofb[:, (ti * 4 + h) * 128:(ti * 4 + h) * 128 + 128] for ti in range(2)],
                          None) for h in range(4)],
                        512, pre=[of_ev])
                    for ti in range(2):
                        col = 4096 * ti + 512 * n
                        if g == 0:
                            rv = em.emit("ve", lambda e, ti=ti, col=col: e.tensor_copy(
                                racc[:, col:col + 512], pb[4 + ti]),
                                waits=[mm_ev] + ([racc_sl.take()] if (n == 0 and ti == 0) else []),
                                bump=VE)
                        else:
                            rv = em.emit("ve", lambda e, ti=ti, col=col: e.tensor_tensor(
                                racc[:, col:col + 512], racc[:, col:col + 512], pb[4 + ti], OP.add),
                                waits=[mm_ev], bump=VE)
                        ps[4 + ti].set(rv)
                qf_sl.set(em.now(PE))
                qz_sl.set(em.now(PE))
            em.barrier()
            for ti, t in enumerate(tiles):
                ev_x = em.dma("sy", x_res, x_in[TOK * t:TOK * (t + 1), :],
                              waits=[xres_sl.take()], sem=LDB)
                rsl = racc[:, 4096 * ti:4096 * (ti + 1)]
                em.emit("ve", lambda e, rsl=rsl: e.tensor_tensor(rsl, rsl, x_res, OP.add),
                        waits=[ev_x, em.now(VE)], bump=VE)
                xres_sl.set(em.now(VE))
                ln_inplace(rsl, sqs, pre=[sqs_sl.take()])
                sqs_sl.set(em.now(SE))
            x1_ev = em.now(VE)
            if upto == 1:
                for ti, t in enumerate(tiles):
                    em.dma("gp", dbg["x1"][TOK * t:TOK * (t + 1), :],
                           racc[:, 4096 * ti:4096 * (ti + 1)], waits=[x1_ev], sem=GPD)
            em.wait("te", xf2_sl.take())
            for ti in range(2):
                xf_evs = transpose_cols(racc[:, 4096 * ti:4096 * (ti + 1)], xf2, 32,
                                        pre=[x1_ev], dst_off=4096 * ti)
            em.wait("te", ps[4].take(), ps[5].take())
            mm_ev = mm_chain(
                (4, 5),
                [(facat[128 * k:128 * (k + 1), :],
                  [xf2[:, 4096 * ti + 128 * k:4096 * ti + 128 * (k + 1)] for ti in range(2)],
                  None) for k in range(32)],
                256, pre=[xf_evs])
            hev = [None, None]
            for ti in range(2):
                hev[ti] = em.emit("ve", lambda e, ti=ti: e.tensor_copy(
                    hbuf[:, 256 * ti:256 * (ti + 1)], pb[4 + ti][:, 0:256]), waits=[mm_ev], bump=VE)
                ps[4 + ti].set(hev[ti])
            for ti in range(2):
                for ch in range(2):
                    b = 6 + ch
                    col = 256 * ti + 128 * ch
                    pe_ev = em.emit("te", tp(pb[b][:, 0:128], hbuf[:, col:col + 128]),
                                    waits=[hev[ti], ps[b].take()], bump=PE)
                    e_ev = em.emit("ve", lambda e, b=b, ti=ti, ch=ch: e.tensor_copy(
                        hfb[:, (ti * 2 + ch) * 128:(ti * 2 + ch) * 128 + 128], pb[b][:, 0:128]),
                        waits=[pe_ev], bump=VE)
                    ps[b].set(e_ev)
            hf_ev = em.now(VE)
            for n in range(8):
                em.wait("te", ps[4].take(), ps[5].take())
                chunks = [(fW[128 * k:128 * (k + 1), 512 * n:512 * (n + 1)],
                           [xf2[:, 4096 * ti + 128 * k:4096 * ti + 128 * (k + 1)] for ti in range(2)],
                           None) for k in range(32)]
                chunks += [(fbcat[128 * k2:128 * (k2 + 1), 512 * n:512 * (n + 1)],
                            [hfb[:, (ti * 2 + k2) * 128:(ti * 2 + k2) * 128 + 128] for ti in range(2)],
                            M_PROBS + k2) for k2 in range(2)]
                mm_ev = mm_chain((4, 5), chunks, 512, pre=[hf_ev])
                for ti in range(2):
                    col = 4096 * ti + 512 * n
                    t1v = em.emit("ve", lambda e, ti=ti, n=n: e.tensor_tensor(
                        sqs[:, 512 * ti:512 * (ti + 1)], pb[4 + ti],
                        fbbc[:, 512 * n:512 * (n + 1)], OP.add),
                        waits=[mm_ev, fbbc_ev] + ([sqs_sl.take()] if (n == 0 and ti == 0) else []),
                        bump=VE)
                    rv = em.emit("ve", lambda e, ti=ti, col=col: e.tensor_tensor(
                        racc[:, col:col + 512], racc[:, col:col + 512],
                        sqs[:, 512 * ti:512 * (ti + 1)], OP.add),
                        waits=[t1v] + ([(GPD, GPD.n)] if (upto == 1 and n == 0 and ti == 0) else []),
                        bump=VE)
                    ps[4 + ti].set(rv)
            sqs_sl.set(em.now(VE))
            em.barrier()
            for ti, t in enumerate(tiles):
                ln_inplace(racc[:, 4096 * ti:4096 * (ti + 1)], sqs,
                           pre=[em.now(VE), sqs_sl.take()])
                sqs_sl.set(em.now(SE))
                spill_evs[t] = em.dma("gp", spx2[TOK * t:TOK * (t + 1), :],
                                      racc[:, 4096 * ti:4096 * (ti + 1)], waits=[em.now(VE)])
            racc_sl.set(spill_evs[tiles[-1]])
            xf2_sl.set(em.now(PE))

    if upto in (1, 2):
        sl = Slot()
        for t in range(NT):
            ev = em.dma("sy", AR2[:, 0:4096], spx2[TOK * t:TOK * (t + 1), :],
                        waits=[spill_evs[NT - 1], sl.take()], sem=LDA)
            sev = em.dma("gp", out[TOK * t:TOK * (t + 1), :], AR2[:, 0:4096], waits=[ev])
            sl.set(sev)
        em.wait("gp", em.now(GP))

    def mm_nk(banks, nk, lhs_fn, wsrc_fn, ncols, pre=()):
        for k in range(nk):
            lhs = lhs_fn(k)
            for bi, bank in enumerate(banks):
                s = SI[0] % 4
                SI[0] += 1
                evw = em.dma("sy", wb[s][:, 0:ncols], wsrc_fn(k, bi), waits=[wsl[s].take()],
                             sem=LDW[s])
                w = [evw] + (list(pre) if (k == 0 and bi == 0) else [])
                em.emit("te", lambda e, bank=bank, lhs=lhs, s=s, k=k, nk=nk, ncols=ncols: e.matmul(
                    pb[bank][:, 0:ncols], lhs, wb[s][:, 0:ncols],
                    start=(k == 0), stop=(k == nk - 1)), waits=w, bump=PE)
                wsl[s].set(em.now(PE))
        return em.now(PE)

    ev_ag1a = None
    qkv = None
    if upto >= 3:
        em.barrier()
        qkv = [AR2[:, 4096 + 1536 * t:4096 + 1536 * (t + 1)] for t in range(NT)]
        xt2sl = Slot()
        xt2sl.set(em.now(VE))
        spill_all = spill_evs[NT - 1]
        xf_guard = spill_all
        qkv_ev = [None] * NT
        for t in range(NT):
            ev_x = em.dma("sy", xt, spx2[TOK * t:TOK * (t + 1), :],
                          waits=[spill_all, xt2sl.take()], sem=LDA)
            em.wait("te", em.now(VE))
            xf_evs = transpose_cols(xt, xf, 32, pre=[ev_x, xf_guard])
            xf_guard = None
            xt2sl.set(em.now(PE))
            em.wait("te", ps[4].take(), ps[5].take(), ps[6].take())
            mm_ev = mm_nk((4, 5, 6), 32,
                          lambda k: xf[:, 128 * k:128 * (k + 1)],
                          lambda k, bi: wqkv[128 * k:128 * (k + 1), 512 * bi:512 * (bi + 1)],
                          512, pre=[xf_evs])
            for bi in range(3):
                qkv_ev[t] = em.emit("ve", lambda e, t=t, bi=bi: e.tensor_copy(
                    qkv[t][:, 512 * bi:512 * (bi + 1)], pb[4 + bi]), waits=[mm_ev], bump=VE)
                ps[4 + bi].set(qkv_ev[t])
            em.dma("gp", ag1a_in[4 * t:4 * (t + 1), :], qkv[t][124:128, :], waits=[qkv_ev[t]])
        if upto == 3:
            for t in range(NT):
                em.dma("gp", dbg["haloin"][4 * t:4 * (t + 1), :], qkv[t][124:128, :], sem=GPD)
        ev_ag1a = em.emit("gp", lambda e: e.collective_compute(
            "AllGather", OP.bypass, replica_groups=RG, ins=[ag1a_in], outs=[ag1a_out]),
            waits=[em.now(GP)], bump=CC)

    if upto == 3:
        sl = Slot()
        for t in range(NT):
            sev = em.dma("gp", dbg["qkv"][TOK * t:TOK * (t + 1), :], qkv[t], waits=[qkv_ev[t]])
        ev = em.dma("sy", AR1[:, 0:1536], ag1a_out, waits=[ev_ag1a])
        hst = em.dma("gp", dbg["halo"], AR1[:, 0:1536], waits=[ev], sem=GPD)
        ev2 = em.dma("sy", AR1[0:32, 2048:3584], ag1a_in, waits=[ev_ag1a])
        hst2 = em.dma("gp", dbg["haloin2"][0:32, :], AR1[0:32, 2048:3584], waits=[ev2, hst], sem=GPD)
        sl.set(hst2)
        for t in range(NT):
            ev = em.dma("sy", AR1[:, 0:4096], spx2[TOK * t:TOK * (t + 1), :], waits=[sl.take()])
            sev = em.dma("gp", out[TOK * t:TOK * (t + 1), :], AR1[:, 0:4096], waits=[ev])
            sl.set(sev)
        em.wait("gp", em.now(GP))

    ev_ag1k = ev_ag1v = None
    x3_evs = [None] * NT
    if upto >= 4:
        qF = AR1[:, 0:8448]
        kF = AR1[:, 8448:10560]
        vF = AR1[:, 10560:12672]
        hl = AR1[:, 12672:14208]
        htm = AR1[:, 14208:15744]
        em.barrier()
        ev_hl = em.dma("sy", hl, ag1a_out, waits=[ev_ag1a, em.now(PE), em.now(VE)], sem=LDC)
        h_ev = None
        for nn in range(3):
            em.wait("te", ps[7].take())
            pe_ev = em.emit("te", lambda e, nn=nn: e.matmul(
                pb[7][0:32, :], c_hsel[:, :], hl[:, 512 * nn:512 * (nn + 1)]),
                waits=[ev_hl, ev_hsel], bump=PE)
            h_ev = em.emit("ve", lambda e, nn=nn: e.tensor_copy(
                htm[0:32, 512 * nn:512 * (nn + 1)], pb[7][0:32, :]), waits=[pe_ev], bump=VE)
            ps[7].set(h_ev)
        qm = AR3[:, 4128:4640]
        km = AR3[:, 4640:5152]
        mean_ev = None
        for t in range(NT):
            qv = qkv[t][:, 0:1024].rearrange("p (h d) -> p d h", h=16, d=64)
            kv_ = qkv[t][:, 1024:1280].rearrange("p (h d) -> p d h", h=4, d=64)
            em.emit("ve", lambda e, t=t, qv=qv: e.reduce_sum(qm[:, 64 * t:64 * (t + 1)], qv, AX),
                    bump=VE)
            mean_ev = em.emit("ve", lambda e, t=t, kv_=kv_: e.reduce_sum(
                km[:, 64 * t:64 * (t + 1)], kv_, AX), bump=VE)
        vtm = AR2[:, 0:2048]
        em.barrier()
        for b in range(NT):
            for ch in range(12):
                if ch < 8:
                    dst, dch = qF, ch
                elif ch < 10:
                    dst, dch = kF, ch - 8
                else:
                    dst, dch = vF, ch - 10
                bk = ch % 2
                pe_ev = em.emit("te", tp(pb[bk][:, 0:128], qkv[b][:, 128 * ch:128 * (ch + 1)]),
                                waits=[ps[bk].take()], bump=PE)
                e_ev = em.emit("ve", lambda e, bk=bk, dst=dst, dch=dch, b=b: e.tensor_copy(
                    dst[:, 1056 * dch + 132 * b + 4:1056 * dch + 132 * b + 132], pb[bk][:, 0:128]),
                    waits=[pe_ev], bump=VE)
                ps[bk].set(e_ev)
            em.emit("ve", lambda e, b=b: e.tensor_copy(
                vtm[:, 256 * b:256 * b + 128], qkv[b][:, 1280:1408]), bump=VE)
        for ch in range(12):
            if ch < 8:
                dst, dch = qF, ch
            elif ch < 10:
                dst, dch = kF, ch - 8
            else:
                dst, dch = vF, ch - 10
            bk = 2 + (ch % 2)
            pe_ev = em.emit("te", tp(pb[bk][:, 0:32], htm[0:32, 128 * ch:128 * (ch + 1)]),
                            waits=[h_ev, ps[bk].take()] if ch == 0 else [ps[bk].take()], bump=PE)
            dv = dst[:, 1056 * dch:1056 * (dch + 1)].rearrange(
                "p (b k) -> p b k", b=8, k=132)[:, :, 0:4]
            sv = pb[bk][:, 0:32].rearrange("p (b k) -> p b k", b=8, k=4)
            e_ev = em.emit("ve", lambda e, dv=dv, sv=sv: e.tensor_copy(dv, sv),
                           waits=[pe_ev], bump=VE)
            ps[bk].set(e_ev)
        trF_ev = em.now(VE)
        for b in range(NT):
            bk = 4 + (b % 2)
            pe_ev = em.emit("te", tp(pb[bk][:, 0:128],
                                     vF[:, 1056 + 132 * b + 3:1056 + 132 * b + 131]),
                            waits=[trF_ev, ps[bk].take()] if b == 0 else [ps[bk].take()], bump=PE)
            e_ev = em.emit("ve", lambda e, bk=bk, b=b: e.tensor_copy(
                vtm[:, 256 * b + 128:256 * b + 256], pb[bk][:, 0:128]), waits=[pe_ev], bump=VE)
            ps[bk].set(e_ev)
            em.dma("gp", ag1v_in[TOK * b:TOK * (b + 1), :], vtm[:, 256 * b:256 * (b + 1)],
                   waits=[em.now(VE)])
        ev_ag1v_in_done = em.now(GP)
        em.barrier()

        c1k = AR3[:, 2048:4128]
        k2tm = AR3[:, 5152:7200]
        rtk = AR3[:, 7200:7456]
        kstg2 = AR3[:, 7456:7712]
        for co in range(2):
            banks = list(range(8))
            em.wait("te", *[ps[b].take() for b in banks])
            nch = 6
            ci_dt = [(ci, dt) for ci in range(2) for dt in range(3)]
            for cidx, (ci, dt) in enumerate(ci_dt):
                s = SI[0] % 4
                SI[0] += 1
                evw = em.dma("sy", wb[s][:, 0:128],
                             kc1_d[256 * dt + 128 * ci:256 * dt + 128 * (ci + 1),
                                   128 * co:128 * (co + 1)], waits=[wsl[s].take()], sem=LDW[s])
                for b in range(8):
                    src = kF[:, 1056 * ci + 132 * b + dt:1056 * ci + 132 * b + dt + 130]
                    w = [evw] + ([trF_ev] if cidx == 0 and b == 0 else [])
                    em.emit("te", lambda e, b=b, src=src, s=s, cidx=cidx, nch=nch: e.matmul(
                        pb[b][:, 0:130], wb[s][:, 0:128], src, start=(cidx == 0),
                        stop=(cidx == nch - 1)), waits=w, bump=(PE if b == 7 else None))
                wsl[s].set(em.now(PE))
            mm_ev = em.now(PE)
            for b in range(8):
                e_ev = em.emit("ve", lambda e, b=b, co=co: e.tensor_scalar(
                    c1k[:, 1040 * co + 130 * b:1040 * co + 130 * b + 130], pb[b][:, 0:130],
                    c_misc[:, M_KC1B + co:M_KC1B + co + 1], None, OP.add),
                    waits=[mm_ev, ev_misc], bump=VE)
                ps[b].set(e_ev)
        c1k_ev = em.now(VE)
        for co in range(2):
            banks = list(range(8))
            em.wait("te", *[ps[b].take() for b in banks])
            ci_dt = [(ci, dt) for ci in range(2) for dt in range(3)]
            nch = 6
            for cidx, (ci, dt) in enumerate(ci_dt):
                s = SI[0] % 4
                SI[0] += 1
                evw = em.dma("sy", wb[s][:, 0:128],
                             kc2_d[256 * dt + 128 * ci:256 * dt + 128 * (ci + 1),
                                   128 * co:128 * (co + 1)], waits=[wsl[s].take()], sem=LDW[s])
                for b in range(8):
                    src = c1k[:, 1040 * ci + 130 * b + dt:1040 * ci + 130 * b + dt + 128]
                    w = [evw] + ([c1k_ev] if cidx == 0 and b == 0 else [])
                    em.emit("te", lambda e, b=b, src=src, s=s, cidx=cidx, nch=nch: e.matmul(
                        pb[b][:, 0:128], wb[s][:, 0:128], src, start=(cidx == 0),
                        stop=(cidx == nch - 1)), waits=w, bump=(PE if b == 7 else None))
                wsl[s].set(em.now(PE))
            mm_ev = em.now(PE)
            for b in range(8):
                e_ev = em.emit("ve", lambda e, b=b, co=co: e.tensor_scalar(
                    kstg2[:, 0:128], pb[b][:, 0:128],
                    c_misc[:, M_KC2B + co:M_KC2B + co + 1], None, OP.add),
                    waits=[mm_ev, ev_misc, kstgsl.take()], bump=VE)
                ps[b].set(e_ev)
                bk = (b % 2)
                pe_ev = em.emit("te", tp(pb[bk][:, 0:128], kstg2[:, 0:128]),
                                waits=[e_ev, ps[bk].take()], bump=PE)
                kstgsl.set(em.now(PE))
                e2 = em.emit("ve", lambda e, bk=bk, b=b, co=co: e.tensor_copy(
                    k2tm[:, 256 * b + 128 * co:256 * b + 128 * co + 128], pb[bk][:, 0:128]),
                    waits=[pe_ev], bump=VE)
                ps[bk].set(e2)
        k2_ev = em.now(VE)
        em.barrier()
        for b in range(NT):
            kb_ = k2tm[:, 256 * b:256 * (b + 1)]
            kvv = kb_.rearrange("p (h d) -> p h d", h=4, d=64)
            qmb = qm[:, 64 * b:64 * (b + 1)].unsqueeze(1).broadcast_to([128, 4, 64])
            em.emit("ve", lambda e, kvv=kvv, qmb=qmb: e.scalar_tensor_tensor(
                kvv, qmb, 0.5 / 16.0, kvv, OP.mult, OP.add), waits=[k2_ev, mean_ev], bump=VE)
            vsq = em.emit("se", lambda e, kb_=kb_: e.activation(
                AR3[:, 7456:7712], kb_, AF.Square), waits=[em.now(VE)], bump=SE)
            sqv = AR3[:, 7456:7712].rearrange("p (h d) -> p h d", h=4, d=64)
            em.emit("ve", lambda e, sqv=sqv: e.reduce_sum(st[:, 26:30].rearrange(
                "p (h o) -> p h o", h=4, o=1), sqv, AX), waits=[vsq], bump=VE)
            vnr = em.emit("se", lambda e: e.activation(st[:, 26:30], st[:, 26:30], AF.Sqrt),
                          waits=[em.now(VE)], bump=SE)
            vmx = em.emit("ve", lambda e: e.tensor_scalar_max(st[:, 26:30], st[:, 26:30], 1e-12),
                          waits=[vnr], bump=VE)
            vri = em.emit("ve", lambda e: e.reciprocal(st[:, 26:30], st[:, 26:30]),
                          waits=[vmx], bump=VE)
            nb = st[:, 26:30].unsqueeze(2).broadcast_to([128, 4, 64])
            em.emit("ve", lambda e, kvv=kvv, nb=nb: e.tensor_tensor(kvv, kvv, nb, OP.mult),
                    waits=[vri], bump=VE)
            vkt = em.emit("ve", lambda e, kb_=kb_: e.tensor_scalar(
                kb_, kb_, c_misc[:, M_KTEMP:M_KTEMP + 1], None, OP.mult),
                waits=[ev_misc, em.now(VE)], bump=VE)
            rp_ev = rope(kb_, AR3[:, 7456:7712], rtk, c_cos1[:, 32 * b:32 * (b + 1)],
                         c_sin1[:, 32 * b:32 * (b + 1)], 4, 64, pre=[vkt, ev_tabs])
            for ch in range(2):
                bk = 2 + ch
                pe_ev = em.emit("te", tp(pb[bk][:, 0:128], AR3[:, 7456 + 128 * ch:7456 + 128 * (ch + 1)]),
                                waits=[rp_ev, ps[bk].take()], bump=PE)
                e2 = em.emit("ve", lambda e, bk=bk: e.tensor_copy(
                    AR3[:, 7712 + 128 * 0:7712 + 128], pb[bk][:, 0:128]), waits=[pe_ev], bump=VE)
                ps[bk].set(e2)
                em.dma("gp", ag1k_in[128 * ch:128 * (ch + 1), TOK * b:TOK * (b + 1)],
                       AR3[:, 7712:7840], waits=[e2], sem=GPK)
                em.wait("ve", em.now(GPK))
        ev_ag1k = em.emit("gp", lambda e: e.collective_compute(
            "AllGather", OP.bypass, replica_groups=RG, ins=[ag1k_in], outs=[ag1k_out]),
            waits=[em.now(GP), em.now(GPK)], bump=CC)
        ev_ag1v = em.emit("gp", lambda e: e.collective_compute(
            "AllGather", OP.bypass, replica_groups=RG, ins=[ag1v_in], outs=[ag1v_out]),
            waits=[ev_ag1v_in_done], bump=CC)

        c1q = AR2[:, 4096:12416]
        q2tm = AR2[:, 12416:13440]
        rtq = AR2[:, 13440:14464]
        sq1 = AR2[:, 14464:15488]
        em.barrier()
        for co in range(8):
            banks = list(range(8))
            em.wait("te", *[ps[b].take() for b in banks])
            ci_dt = [(ci, dt) for ci in range(8) for dt in range(3)]
            nch = 24
            for cidx, (ci, dt) in enumerate(ci_dt):
                s = SI[0] % 4
                SI[0] += 1
                evw = em.dma("sy", wb[s][:, 0:128],
                             qc1_d[1024 * dt + 128 * ci:1024 * dt + 128 * (ci + 1),
                                   128 * co:128 * (co + 1)], waits=[wsl[s].take()], sem=LDW[s])
                for b in range(8):
                    src = qF[:, 1056 * ci + 132 * b + dt:1056 * ci + 132 * b + dt + 130]
                    w = [evw] + ([trF_ev] if cidx == 0 and b == 0 else [])
                    em.emit("te", lambda e, b=b, src=src, s=s, cidx=cidx, nch=nch: e.matmul(
                        pb[b][:, 0:130], wb[s][:, 0:128], src, start=(cidx == 0),
                        stop=(cidx == nch - 1)), waits=w, bump=(PE if b == 7 else None))
                wsl[s].set(em.now(PE))
            mm_ev = em.now(PE)
            for b in range(8):
                e_ev = em.emit("ve", lambda e, b=b, co=co: e.tensor_scalar(
                    c1q[:, 1040 * co + 130 * b:1040 * co + 130 * b + 130], pb[b][:, 0:130],
                    c_misc[:, M_QC1B + co:M_QC1B + co + 1], None, OP.add),
                    waits=[mm_ev, ev_misc], bump=VE)
                ps[b].set(e_ev)
        c1q_ev = em.now(VE)
        c2q = AR1[:, 8192:16384]
        for co in range(8):
            banks = list(range(8))
            em.wait("te", *[ps[b].take() for b in banks])
            ci_dt = [(ci, dt) for ci in range(8) for dt in range(3)]
            nch = 24
            for cidx, (ci, dt) in enumerate(ci_dt):
                s = SI[0] % 4
                SI[0] += 1
                evw = em.dma("sy", wb[s][:, 0:128],
                             qc2_d[1024 * dt + 128 * ci:1024 * dt + 128 * (ci + 1),
                                   128 * co:128 * (co + 1)], waits=[wsl[s].take()], sem=LDW[s])
                for b in range(8):
                    src = c1q[:, 1040 * ci + 130 * b + dt:1040 * ci + 130 * b + dt + 128]
                    w = [evw] + ([c1q_ev] if cidx == 0 and b == 0 else [])
                    em.emit("te", lambda e, b=b, src=src, s=s, cidx=cidx, nch=nch: e.matmul(
                        pb[b][:, 0:128], wb[s][:, 0:128], src, start=(cidx == 0),
                        stop=(cidx == nch - 1)), waits=w, bump=(PE if b == 7 else None))
                wsl[s].set(em.now(PE))
            mm_ev = em.now(PE)
            for b in range(8):
                e_ev = em.emit("ve", lambda e, b=b, co=co: e.tensor_scalar(
                    c2q[:, 1024 * b + 128 * co:1024 * b + 128 * co + 128], pb[b][:, 0:128],
                    c_misc[:, M_QC2B + co:M_QC2B + co + 1], None, OP.add),
                    waits=[mm_ev, ev_misc], bump=VE)
                ps[b].set(e_ev)
        c2q_ev = em.now(VE)
        qf1 = AR1[:, 0:8192]
        em.barrier()
        for b in range(NT):
            for ch in range(8):
                bk = ch % 2
                pe_ev = em.emit("te", tp(pb[bk][:, 0:128], c2q[:, 1024 * b + 128 * ch:1024 * b + 128 * (ch + 1)]),
                                waits=[c2q_ev, ps[bk].take()] if (b == 0 and ch == 0) else [ps[bk].take()],
                                bump=PE)
                e_ev = em.emit("ve", lambda e, bk=bk, ch=ch: e.tensor_copy(
                    q2tm[:, 128 * ch:128 * (ch + 1)], pb[bk][:, 0:128]), waits=[pe_ev], bump=VE)
                ps[bk].set(e_ev)
            qvv = q2tm.rearrange("p (h d) -> p h d", h=16, d=64)
            kmb = km[:, 64 * b:64 * (b + 1)].unsqueeze(1).broadcast_to([128, 16, 64])
            em.emit("ve", lambda e, qvv=qvv, kmb=kmb: e.scalar_tensor_tensor(
                qvv, kmb, 0.5 / 4.0, qvv, OP.mult, OP.add), waits=[em.now(VE), mean_ev], bump=VE)
            vsq = em.emit("se", lambda e: e.activation(sq1, q2tm, AF.Square),
                          waits=[em.now(VE)], bump=SE)
            sqv = sq1.rearrange("p (h d) -> p h d", h=16, d=64)
            em.emit("ve", lambda e, sqv=sqv: e.reduce_sum(
                st[:, 0:16].rearrange("p (h o) -> p h o", h=16, o=1), sqv, AX),
                waits=[vsq], bump=VE)
            vnr = em.emit("se", lambda e: e.activation(st[:, 0:16], st[:, 0:16], AF.Sqrt),
                          waits=[em.now(VE)], bump=SE)
            vmx = em.emit("ve", lambda e: e.tensor_scalar_max(st[:, 0:16], st[:, 0:16], 1e-12),
                          waits=[vnr], bump=VE)
            vri = em.emit("ve", lambda e: e.reciprocal(st[:, 0:16], st[:, 0:16]),
                          waits=[vmx], bump=VE)
            nb = st[:, 0:16].unsqueeze(2).broadcast_to([128, 16, 64])
            em.emit("ve", lambda e, qvv=qvv, nb=nb: e.tensor_tensor(qvv, qvv, nb, OP.mult),
                    waits=[vri], bump=VE)
            rp_ev = rope(q2tm, sq1, rtq, c_cos1[:, 32 * b:32 * (b + 1)],
                         c_sin1[:, 32 * b:32 * (b + 1)], 16, 64, pre=[em.now(VE), ev_tabs])
            qperm_src = sq1.rearrange("p (a b2 c4 jt) -> p a b2 c4 jt", a=2, b2=2, c4=4, jt=64)
            qperm_dst = q2tm.rearrange("p (a c4 b2 jt) -> p a b2 c4 jt", a=2, c4=4, b2=2, jt=64)
            qp_ev = em.emit("ve", lambda e, qperm_dst=qperm_dst, qperm_src=qperm_src: e.tensor_copy(
                qperm_dst, qperm_src), waits=[rp_ev], bump=VE)
            for chn in range(8):
                bk = 2 + (chn % 2)
                pe_ev = em.emit("te", tp(pb[bk][:, 0:128], q2tm[:, 128 * chn:128 * (chn + 1)]),
                                waits=[qp_ev, ps[bk].take()], bump=PE)
                e_ev = em.emit("ve", lambda e, bk=bk, chn=chn, b=b: e.tensor_copy(
                    qf1[:, 1024 * chn + 128 * b:1024 * chn + 128 * b + 128],
                    pb[bk][:, 0:128]), waits=[pe_ev], bump=VE)
                ps[bk].set(e_ev)
        qf1_ev = em.now(VE)

        O1 = AR2[:, 0:8192]
        kt1 = [AR3[:, 2048 + 1024 * i:2048 + 1024 * (i + 1)] for i in range(2)]
        vt1 = [AR3[:, 4096 + 1024 * i:4096 + 1024 * (i + 1)] for i in range(2)]
        pbuf1 = [AR3[:, 6144 + 512 * i:6144 + 512 * (i + 1)] for i in range(2)]
        ptsb1 = [AR3[:, 7168 + 512 * i:7168 + 512 * (i + 1)] for i in range(2)]
        kt1_sl = [Slot(), Slot()]
        vt1_sl = [Slot(), Slot()]
        pbuf1_sl = [Slot(), Slot()]
        ptsb1_sl = [Slot(), Slot()]
        ssb1_sl = [Slot(), Slot()]
        O1_sl = Slot()
        O1_sl.set(ev_ag1v_in_done)
        em.barrier()
        ssb1 = [AR2[:, 8192 + 512 * i:8192 + 512 * (i + 1)] for i in range(2)]
        unit = 0
        first_kv1 = True
        for j in range(NT):
            par = j % 2
            kt_ev = None
            for kp in range(2):
                for hh in range(2):
                    kvh = 2 * kp + hh
                    for rr in range(4):
                        w = []
                        if kp == 0 and hh == 0 and rr == 0:
                            w = [kt1_sl[par].take()]
                            if first_kv1:
                                w += [ev_ag1k, ev_ag1v]
                        kt_ev = em.dma("sy", kt1[par][64 * hh:64 * (hh + 1),
                                                      512 * kp + 128 * rr:512 * kp + 128 * (rr + 1)],
                                       ag1k_out[256 * rr + 64 * kvh:256 * rr + 64 * (kvh + 1),
                                                TOK * j:TOK * (j + 1)], waits=w, sem=LDK[par])
            vt_ev = None
            for rr in range(4):
                vt_ev = em.dma("sy", vt1[par][:, 256 * rr:256 * (rr + 1)],
                               ag1v_out[1024 * rr + TOK * j:1024 * rr + TOK * (j + 1), :],
                               waits=[vt1_sl[par].take()] if rr == 0 else [], sem=LDV[par])
            first_kv1 = False
            for t in range(j, NT):
                for qh in range(16):
                    kvh = qh // 4
                    kp, hh = kvh // 2, kvh % 2
                    if qh < 8:
                        chn, half = qh % 4, qh // 4
                    else:
                        chn, half = 4 + (qh % 4), (qh - 8) // 4
                    up = unit % 2
                    sb_, ob_, vb_ = up, 2 + up, 4 + up
                    sidx = 2 * (t * 16 + qh)
                    lhs = qf1[64 * half:64 * (half + 1), 1024 * chn + 128 * t:1024 * chn + 128 * t + 128]
                    rhs = kt1[par][64 * hh:64 * (hh + 1), 512 * kp:512 * (kp + 1)]
                    tpos = (64 * half, 0)
                    s_ev = em.emit("te", lambda e, sb_=sb_, lhs=lhs, rhs=rhs, tpos=tpos: e.matmul(
                        pb[sb_], lhs, rhs, tile_position=tpos),
                        waits=[kt_ev, qf1_ev, ps[sb_].take()], bump=PE)
                    kt1_sl[par].set(em.now(PE))
                    diag = (j == t)
                    if diag:
                        src = ssb1[up]
                        src_ev = em.emit("ve", lambda e, up=up, sb_=sb_: e.tensor_tensor(
                            ssb1[up], pb[sb_], c_mask[:, :], OP.add),
                            waits=[s_ev, ssb1_sl[up].take(), ev_mask], bump=VE)
                        ps[sb_].set(src_ev)
                    else:
                        src = pb[sb_]
                        src_ev = s_ev
                    first = (j == 0)
                    if first:
                        vm = em.emit("ve", lambda e, src=src, sidx=sidx: e.reduce_max(
                            stm1[:, sidx:sidx + 1], src, AX), waits=[src_ev], bump=VE)
                        vnb = em.emit("ve", lambda e, sidx=sidx: e.tensor_scalar_mul(
                            st[:, 19:20], stm1[:, sidx:sidx + 1], -SC1), waits=[vm], bump=VE)
                        pexp = em.emit("se", lambda e, src=src, up=up: e.activation(
                            pbuf1[up], src, AF.Exp, bias=st[:, 19:20], scale=SC1,
                            accum_out=st[:, 20:21]), waits=[vnb, pbuf1_sl[up].take()], bump=SE)
                        em.emit("ve", lambda e, sidx=sidx: e.tensor_copy(
                            stm1[:, sidx + 1:sidx + 2], st[:, 20:21]), waits=[pexp], bump=VE)
                        salpha = None
                    else:
                        vm = em.emit("ve", lambda e, src=src: e.reduce_max(
                            st[:, 16:17], src, AX), waits=[src_ev], bump=VE)
                        em.emit("ve", lambda e, sidx=sidx: e.tensor_tensor(
                            st[:, 17:18], st[:, 16:17], stm1[:, sidx:sidx + 1], OP.max), bump=VE)
                        vd = em.emit("ve", lambda e, sidx=sidx: e.tensor_tensor(
                            st[:, 18:19], stm1[:, sidx:sidx + 1], st[:, 17:18], OP.subtract), bump=VE)
                        vmu = em.emit("ve", lambda e, sidx=sidx: e.tensor_copy(
                            stm1[:, sidx:sidx + 1], st[:, 17:18]), bump=VE)
                        salpha = em.emit("se", lambda e: e.activation(
                            st[:, 25:26], st[:, 18:19], AF.Exp, scale=SC1), waits=[vd], bump=SE)
                        vnb = em.emit("ve", lambda e: e.tensor_scalar_mul(
                            st[:, 19:20], st[:, 17:18], -SC1), waits=[vmu], bump=VE)
                        pexp = em.emit("se", lambda e, src=src, up=up: e.activation(
                            pbuf1[up], src, AF.Exp, bias=st[:, 19:20], scale=SC1,
                            accum_out=st[:, 20:21]), waits=[vnb, pbuf1_sl[up].take()], bump=SE)
                        em.emit("ve", lambda e, sidx=sidx: e.scalar_tensor_tensor(
                            stm1[:, sidx + 1:sidx + 2], stm1[:, sidx + 1:sidx + 2],
                            st[:, 25:26], st[:, 20:21], OP.mult, OP.add),
                            waits=[pexp, salpha], bump=VE)
                    if not diag:
                        ps[sb_].set((SE, SE.n))
                    else:
                        ssb1_sl[up].set((SE, SE.n))
                    em.wait("te", ps[ob_].take())
                    for s4 in range(4):
                        em.emit("te", tp(pb[ob_][:, 128 * s4:128 * (s4 + 1)],
                                         pbuf1[up][:, 128 * s4:128 * (s4 + 1)],
                                         start=(s4 == 0), stop=(s4 == 3)),
                                waits=[pexp] if s4 == 0 else [], bump=(PE if s4 == 3 else None))
                    pt_pe = em.now(PE)
                    pbuf1_sl[up].set(pt_pe)
                    ptev = em.emit("ve", lambda e, ob_=ob_, up=up: e.tensor_copy(
                        ptsb1[up], pb[ob_]), waits=[pt_pe, ptsb1_sl[up].take()], bump=VE)
                    ps[ob_].set(ptev)
                    em.wait("te", ps[vb_].take())
                    for s4 in range(4):
                        em.emit("te", lambda e, vb_=vb_, up=up, s4=s4, par=par, kvh=kvh: e.matmul(
                            pb[vb_][:, 0:64], ptsb1[up][:, 128 * s4:128 * (s4 + 1)],
                            vt1[par][:, 256 * s4 + 64 * kvh:256 * s4 + 64 * (kvh + 1)],
                            start=(s4 == 0), stop=(s4 == 3)),
                            waits=[ptev, vt_ev] if s4 == 0 else [], bump=(PE if s4 == 3 else None))
                    pv_pe = em.now(PE)
                    ptsb1_sl[up].set(pv_pe)
                    vt1_sl[par].set(pv_pe)
                    oc = 1024 * t + 64 * qh
                    if first:
                        ovev = em.emit("ve", lambda e, vb_=vb_, oc=oc: e.tensor_copy(
                            O1[:, oc:oc + 64], pb[vb_][:, 0:64]),
                            waits=[pv_pe, O1_sl.take()], bump=VE)
                    else:
                        ovev = em.emit("ve", lambda e, vb_=vb_, oc=oc: e.scalar_tensor_tensor(
                            O1[:, oc:oc + 64], O1[:, oc:oc + 64], st[:, 25:26],
                            pb[vb_][:, 0:64], OP.mult, OP.add), waits=[pv_pe, salpha], bump=VE)
                    ps[vb_].set(ovev)
                    unit += 1
        em.barrier()
        vrec = em.emit("ve", lambda e: e.reciprocal(stm1i[:, :], stm1[:, 1:256:2]),
                       waits=[em.now(VE)], bump=VE)
        for t in range(NT):
            for qh in range(16):
                s_ = t * 16 + qh
                oc = 1024 * t + 64 * qh
                em.emit("ve", lambda e, oc=oc, s_=s_: e.tensor_scalar(
                    O1[:, oc:oc + 64], O1[:, oc:oc + 64], stm1i[:, s_:s_ + 1],
                    None, OP.mult), waits=[vrec] if s_ == 0 else [], bump=VE)
        odiv_ev = em.now(VE)
        ofb1 = AR3[:, 6144:7168]
        x2rl = AR2[:, 12288:16384]
        x3buf = AR2[:, 8192:12288]
        x2sl_ = Slot()
        x3sl_ = Slot()
        em.barrier()
        for t in range(NT):
            for ch in range(8):
                bk = ch % 2
                pe_ev = em.emit("te", tp(pb[bk][:, 0:128], O1[:, 1024 * t + 128 * ch:1024 * t + 128 * (ch + 1)]),
                                waits=[odiv_ev, ps[bk].take()] if (t == 0 and ch == 0) else [ps[bk].take()],
                                bump=PE)
                e_ev = em.emit("ve", lambda e, bk=bk, ch=ch: e.tensor_copy(
                    ofb1[:, 128 * ch:128 * (ch + 1)], pb[bk][:, 0:128]), waits=[pe_ev], bump=VE)
                ps[bk].set(e_ev)
            of_ev = em.now(VE)
            ev_x2 = em.dma("sy", x2rl, spx2[TOK * t:TOK * (t + 1), :],
                           waits=[x2sl_.take()], sem=LDA)
            for n in range(8):
                em.wait("te", ps[4].take())
                mm_ev = mm_chain(
                    (4,),
                    [(wot[128 * ch:128 * (ch + 1), 512 * n:512 * (n + 1)],
                      [ofb1[:, 128 * ch:128 * (ch + 1)]], None) for ch in range(8)],
                    512, pre=[of_ev])
                rv = em.emit("ve", lambda e, n=n: e.tensor_tensor(
                    x3buf[:, 512 * n:512 * (n + 1)], pb[4], x2rl[:, 512 * n:512 * (n + 1)],
                    OP.add), waits=[mm_ev, ev_x2] + ([x3sl_.take()] if n == 0 else []), bump=VE)
                ps[4].set(rv)
            x2sl_.set(em.now(VE))
            ln_inplace(x3buf, AR1[:, 0:4096], pre=[em.now(VE)])
            x3_evs[t] = em.dma("gp", spx3[TOK * t:TOK * (t + 1), :], x3buf, waits=[em.now(VE)])
            x3sl_.set(x3_evs[t])

    if upto == 4:
        for t in range(NT):
            em.dma("gp", dbg["O1"][TOK * t:TOK * (t + 1), :], O1[:, 1024 * t:1024 * (t + 1)],
                   waits=[odiv_ev], sem=GPD)
        sl = Slot()
        sl.set(em.now(PE))
        for r_ in range(8):
            ev = em.dma("sy", AR1[:, 4096:5120], ag1k_out[128 * r_:128 * (r_ + 1), :],
                        waits=[ev_ag1k, ev_ag1v, sl.take()], sem=LDB)
            sev = em.dma("gp", dbg["agk1"][128 * r_:128 * (r_ + 1), :], AR1[:, 4096:5120],
                         waits=[ev], sem=GPD)
            sl.set(sev)
        for r_ in range(32):
            ev = em.dma("sy", AR1[:, 4096:4352], ag1v_out[128 * r_:128 * (r_ + 1), :],
                        waits=[sl.take()], sem=LDB)
            sev = em.dma("gp", dbg["agv1"][128 * r_:128 * (r_ + 1), :], AR1[:, 4096:4352],
                         waits=[ev], sem=GPD)
            sl.set(sev)
        for t in range(NT):
            ev = em.dma("sy", AR1[:, 0:4096], spx3[TOK * t:TOK * (t + 1), :],
                        waits=[x3_evs[NT - 1], sl.take()], sem=LDA)
            sev = em.dma("gp", out[TOK * t:TOK * (t + 1), :], AR1[:, 0:4096], waits=[ev])
            sl.set(sev)
        em.wait("gp", em.now(GP))

    if upto >= 5:
        em.barrier()
        em.wait("ve", x3_evs[NT - 1])
        em.wait("se", x3_evs[NT - 1])
        gam = AR1[:, 0:4096]
        bet = AR1[:, 4096:8192]
        lbi = AR1[:, 8192:12288]
        em.dma("sy", gam, lgam_d, waits=[em.now(PE), em.now(SE), em.now(VE)], sem=LDC)
        em.dma("sy", bet, lbet_d, sem=LDC)
        ev_l = em.dma("sy", lbi, lbias_d, sem=LDC)
        ev_g = ev_b = ev_l
        x3t = AR2[:, 0:4096]
        hp = AR2[:, 4096:8192]
        G_tm = AR2[:, 8192:12288]
        y_tm = AR2[:, 12288:16384]
        tmp5 = AR3[:, 2048:6144]
        stg1 = AR3[:, 6144:6272]
        stg2 = AR3[:, 6272:6400]
        x3tsl = Slot()
        x3tsl.set(x3_evs[NT - 1])
        for t in range(NT):
            ev_x3 = em.dma("sy", x3t, spx3[TOK * t:TOK * (t + 1), :],
                           waits=[x3_evs[NT - 1], x3tsl.take()], sem=LDA)
            em.emit("ve", lambda e: e.reduce_sum(st[:, 21:22], x3t, AX),
                    waits=[ev_x3, em.now(VE)], bump=VE)
            vm = em.emit("ve", lambda e: e.tensor_scalar_mul(st[:, 22:23], st[:, 21:22], 1.0 / D),
                         bump=VE)
            vs = em.emit("ve", lambda e: e.tensor_scalar(tmp5, x3t, st[:, 22:23], None, OP.subtract),
                         waits=[vm], bump=VE)
            em.emit("se", lambda e: e.activation(y_tm, tmp5, AF.Square, accum_out=st[:, 23:24]),
                    waits=[vs], bump=SE)
            sstd = em.emit("se", lambda e: e.activation(st[:, 24:25], st[:, 23:24], AF.Sqrt,
                                                        bias=c_misc[:, 23:24], scale=1.0 / D), bump=SE)
            vrec5 = em.emit("ve", lambda e: e.reciprocal(st[:, 21:22], st[:, 24:25]),
                            waits=[sstd], bump=VE)
            em.emit("ve", lambda e: e.tensor_scalar(tmp5, tmp5, st[:, 21:22], None, OP.mult),
                    waits=[vrec5], bump=VE)
            em.emit("ve", lambda e: e.tensor_tensor(tmp5, tmp5, gam, OP.mult),
                    waits=[ev_g], bump=VE)
            hpv = hp.rearrange("p (a q) -> p q a", a=64, q=64)
            t5v = tmp5.rearrange("p (q a) -> p q a", q=64, a=64)
            btv = bet.rearrange("p (q a) -> p q a", q=64, a=64)
            hperm = em.emit("ve", lambda e, hpv=hpv, t5v=t5v, btv=btv: e.tensor_tensor(
                hpv, t5v, btv, OP.add), waits=[ev_b], bump=VE)
            stg1_sl = Slot()
            stg2_sl = Slot()
            for step, (src_tm, dst_tm, wmat) in enumerate(
                    ((hp, G_tm, c_lB), (G_tm, y_tm, c_lA))):
                pre_ev = hperm if step == 0 else em.now(VE)
                for ch in range(32):
                    b0 = ch % 2
                    pe1 = em.emit("te", tp(pb[b0][:, 0:128], src_tm[:, 128 * ch:128 * (ch + 1)]),
                                  waits=[pre_ev, ps[b0].take()] if ch == 0 else [ps[b0].take()],
                                  bump=PE)
                    e1 = em.emit("ve", lambda e, b0=b0: e.tensor_copy(stg1, pb[b0][:, 0:128]),
                                 waits=[pe1, stg1_sl.take()], bump=VE)
                    ps[b0].set(e1)
                    em.wait("te", ps[2].take())
                    em.emit("te", lambda e, wmat=wmat: e.matmul(
                        pb[2][0:64, 0:128], wmat[0:64, :], stg1[0:64, :], tile_position=(0, 0),
                        start=True, stop=False), waits=[e1, ev_lab], bump=None)
                    mm2 = em.emit("te", lambda e, wmat=wmat: e.matmul(
                        pb[2][64:128, 0:128], wmat[64:128, :], stg1[64:128, :], tile_position=(64, 64),
                        start=True, stop=True), bump=PE)
                    stg1_sl.set(em.now(PE))
                    e2 = em.emit("ve", lambda e: e.tensor_copy(stg2, pb[2][:, 0:128]),
                                 waits=[mm2, stg2_sl.take()], bump=VE)
                    ps[2].set(e2)
                    b3 = 3
                    pe3 = em.emit("te", tp(pb[b3][:, 0:128], stg2),
                                  waits=[e2, ps[b3].take()], bump=PE)
                    stg2_sl.set(em.now(PE))
                    if step == 0:
                        dv = dst_tm.rearrange("p (q a) -> p q a", q=64, a=64)[:, :, 2 * ch:2 * ch + 2]
                        sv = pb[b3][:, 0:128].rearrange("p (a q) -> p q a", a=2, q=64)
                    else:
                        dv = dst_tm[:, 128 * ch:128 * (ch + 1)]
                        sv = pb[b3][:, 0:128]
                    e3 = em.emit("ve", lambda e, dv=dv, sv=sv: e.tensor_copy(dv, sv),
                                 waits=[pe3], bump=VE)
                    ps[b3].set(e3)
            y_ev = em.now(VE)
            gl = em.emit("se", lambda e: e.activation(tmp5, y_tm, AF.Gelu), waits=[y_ev], bump=SE)
            em.emit("ve", lambda e: e.tensor_tensor(tmp5, tmp5, lbi, OP.add),
                    waits=[gl, ev_l], bump=VE)
            em.emit("ve", lambda e: e.tensor_tensor(x3t, x3t, tmp5, OP.add), bump=VE)
            ln_inplace(x3t, tmp5, pre=[em.now(VE)])
            oev = em.dma("gp", out[TOK * t:TOK * (t + 1), :], x3t, waits=[em.now(VE)])
            x3tsl.set(oev)
        em.wait("gp", em.now(GP), em.now(GPK))

    with nc.Block() as block:
        @block.sync
        def _(e):
            for f in em.qs["sy"]:
                f(e)

        @block.tensor
        def _(e):
            for f in em.qs["te"]:
                f(e)

        @block.vector
        def _(e):
            for f in em.qs["ve"]:
                f(e)

        @block.scalar
        def _(e):
            for f in em.qs["se"]:
                f(e)

        @block.gpsimd
        def _(e):
            for f in em.qs["gp"]:
                f(e)

    return nc, dbg


def _host_inputs(inputs, core):
    b, c = core // 4, core % 4
    blocks = [4 * i + c for i in range(NT)]
    x = np.ascontiguousarray(
        np.asarray(inputs["L0_attn_x"][b]).reshape(32, TOK, D)[blocks].reshape(NT * TOK, D),
        dtype=np.float32)
    rows = np.concatenate([np.arange(TOK) + TOK * g for g in blocks])

    def ropetab(hd):
        import jax
        import jax.numpy as jnp
        with jax.default_device(jax.devices("cpu")[0]):
            freqs = 1.0 / 10000.0 ** (jnp.arange(0, hd, 2, dtype=jnp.float32) / hd)
            t = jnp.arange(4096, dtype=jnp.float32)
            ang = jnp.outer(t, freqs)
            cos = np.asarray(jnp.cos(ang), dtype=np.float32)[rows]
            sin = np.asarray(jnp.sin(ang), dtype=np.float32)[rows]
        z = np.zeros_like(cos)
        return {"ch": np.ascontiguousarray(cos), "cl": z,
                "sh": np.ascontiguousarray(sin), "sl": z.copy()}

    t0, t1 = ropetab(128), ropetab(64)
    mask = np.zeros((128, 512), dtype=np.float32)
    for q_ in range(128):
        mask[q_, 128 * c + q_ + 1:] = NEGM
    hs = np.zeros((128, 32), dtype=np.float32)
    for i in range(NT):
        gblk = 4 * i + c
        if gblk == 0:
            continue
        prev = gblk - 1
        rsrc, jsrc = prev % 4, prev // 4
        for k in range(4):
            hs[32 * rsrc + 4 * jsrc + k, 4 * i + k] = 1.0
    return dict(
        x=x,
        r0ch=t0["ch"], r0cl=t0["cl"], r0sh=t0["sh"], r0sl=t0["sl"],
        r1ch=t1["ch"], r1cl=t1["cl"], r1sh=t1["sh"], r1sl=t1["sl"],
        maskin=mask, halosel=hs,
    )


def _host_shared(inputs):
    f32 = lambda a: np.ascontiguousarray(np.asarray(a), dtype=np.float32)
    A = f32(inputs["L0_ffn_A"]).reshape(256, D)
    Bw = f32(inputs["L0_ffn_B"])
    Bcat = np.ascontiguousarray(Bw.transpose(0, 2, 1).reshape(256, D))
    probs_rep = np.repeat(f32(inputs["L0_ffn_probs"]), 16)[:, None]
    probs_rep = np.ascontiguousarray(probs_rep)

    def dense_conv(wgrp, groups):
        C = wgrp.shape[0]
        gs = C // groups
        wd = np.zeros((3 * C, C), dtype=np.float32)
        for co in range(C):
            gidx = co // gs
            for dt in range(3):
                wd[dt * C + gidx * gs:(dt * C) + (gidx + 1) * gs, co] = wgrp[co, :, dt]
        return np.ascontiguousarray(wd)

    def dense_full(w):
        C = w.shape[0]
        wd = np.zeros((3 * C, C), dtype=np.float32)
        for dt in range(3):
            wd[dt * C:(dt + 1) * C, :] = w[:, :, dt].T
        return np.ascontiguousarray(wd)

    bc = lambda v: np.ascontiguousarray(np.broadcast_to(f32(v)[None, :], (128, D)))
    qc1w = f32(inputs["L1_attn_qc1_w"])
    qc2w = f32(inputs["L1_attn_qc2_w"])
    kc1w = f32(inputs["L1_attn_kc1_w"])
    kc2w = f32(inputs["L1_attn_kc2_w"])
    lAv = f32(inputs["L1_ffn_A"])
    lBv = f32(inputs["L1_ffn_B"])
    return dict(
        wq=f32(inputs["L0_attn_wq"]),
        wk=f32(inputs["L0_attn_wk"]),
        wv=f32(inputs["L0_attn_wv"]),
        wo=f32(inputs["L0_attn_wo"]),
        fW=np.ascontiguousarray(f32(inputs["L0_ffn_W"]).T),
        facat=np.ascontiguousarray(A.T),
        fbcat=Bcat,
        probsr=probs_rep,
        fbbc=bc(inputs["L0_ffn_b"]),
        wqkv=np.ascontiguousarray(f32(inputs["L1_attn_w_qkv"]).T),
        wot=np.ascontiguousarray(f32(inputs["L1_attn_w_o"]).T),
        qc1=dense_conv(qc1w, 16),
        qc2=dense_full(qc2w),
        kc1=dense_conv(kc1w, 4),
        kc2=dense_full(kc2w),
        qc1b=f32(inputs["L1_attn_qc1_b"])[:, None],
        qc2b=f32(inputs["L1_attn_qc2_b"])[:, None],
        kc1b=f32(inputs["L1_attn_kc1_b"])[:, None],
        kc2b=f32(inputs["L1_attn_kc2_b"])[:, None],
        ktemp=np.ascontiguousarray(np.full((128, 1), float(np.asarray(inputs["L1_attn_key_temp"])[0]),
                                           dtype=np.float32)),
        epsv=np.full((128, 1), EPS, dtype=np.float32),
        lA=np.ascontiguousarray(np.concatenate([lAv, lAv], axis=0)),
        lB=np.ascontiguousarray(np.concatenate([lBv, lBv], axis=0)),
        lbias=bc(inputs["L1_ffn_bias"]),
        lgam=bc(inputs["L1_ffn_ln_gamma"]),
        lbet=bc(inputs["L1_ffn_ln_beta"]),
        identin=np.eye(128, dtype=np.float32),
    )


def kernel(**inputs):
    nc, dbg = build_program(upto=5)
    shared = _host_shared(inputs)
    in_maps = []
    for core in range(8):
        m = dict(shared)
        m.update(_host_inputs(inputs, core))
        in_maps.append(m)
    res = run_bass_kernel_spmd(nc, in_maps, core_ids=list(range(8)))
    outp = np.empty((2, 4096, D), dtype=np.float32)
    for core in range(8):
        b, c = core // 4, core % 4
        o = res.results[core]["out"].reshape(NT, TOK, D)
        for i in range(NT):
            g = 4 * i + c
            outp[b, TOK * g:TOK * (g + 1), :] = o[i]
    return outp

